# version 02
# baseline (speedup 1.0000x reference)
import numpy as np
import ml_dtypes
from contextlib import ExitStack

import concourse.bass as bass
import concourse.mybir as mybir
from concourse.bass_utils import run_bass_kernel_spmd

F32 = mybir.dt.float32
BF16 = mybir.dt.bfloat16
I32 = mybir.dt.int32
AF = mybir.ActivationFunctionType
ALU = mybir.AluOpType

NC = 8
HID = 4096
DIN = 8192
CS = 256
TG = 1024
NGRP = 8
NCHUNK = 4
T_ALL = 8192
EPS = np.float32(1e-5)
NFT = 19
FT_B = 8
FT_C = 9
FT_G0 = 10
FT_DT = 18
NCONV_FT = 10
RAWW = TG + 3
KCONV = 4
HSZ = 1408
NHT = HSZ // 128
HIDDEN = 11008


class Bld:
    def __init__(self):
        self.nc = bass.Bass("TRN2", target_bir_lowering=False)
        self.es = ExitStack()
        self.sems = {}
        self.cnt = {}

    def sem(self, name):
        h = self.es.enter_context(self.nc.semaphore(name))
        self.sems[name] = h
        self.cnt[name] = 0

    def inc(self, inst, name, n=1):
        inst.then_inc(self.sems[name], n)
        self.cnt[name] += n
        return self.cnt[name]

    def wait(self, eng, name, val=None):
        eng.wait_ge(self.sems[name], self.cnt[name] if val is None else val)

    def sb(self, name, shape, dt):
        return self.es.enter_context(self.nc.sbuf_tensor(name, shape, dt))

    def ps(self, name, shape, dt):
        return self.es.enter_context(self.nc.psum_tensor(name, shape, dt))


def build_program(ngroups=NGRP, debug=(), nchunks=NCHUNK):
    B = Bld()
    nc = B.nc

    x_in = nc.dram_tensor("x_in", [T_ALL, HID], F32, kind="ExternalInput").ap()
    xrow_in = nc.dram_tensor("xrow_in", [NGRP * 128, HID], F32, kind="ExternalInput").ap()
    wip_in = nc.dram_tensor("wip_in", [NFT * 128, HID], F32, kind="ExternalInput").ap()
    wo_in = nc.dram_tensor("wo_in", [HID, 1024], F32, kind="ExternalInput").ap()
    cvw_in = nc.dram_tensor("cvw_in", [NCONV_FT * 128, KCONV], F32, kind="ExternalInput").ap()
    cvb_in = nc.dram_tensor("cvb_in", [NCONV_FT * 128, 1], F32, kind="ExternalInput").ap()
    dtb_in = nc.dram_tensor("dtb_in", [1, 128], F32, kind="ExternalInput").ap()
    alog_in = nc.dram_tensor("alog_in", [1, 128], F32, kind="ExternalInput").ap()
    drep_in = nc.dram_tensor("drep_in", [1, 1024], F32, kind="ExternalInput").ap()
    nw_in = nc.dram_tensor("nw_in", [1, 1024], F32, kind="ExternalInput").ap()
    w1p_in = nc.dram_tensor("w1p_in", [32 * HSZ, 64], I32, kind="ExternalInput").ap()
    w3p_in = nc.dram_tensor("w3p_in", [32 * HSZ, 64], I32, kind="ExternalInput").ap()
    w2p_in = nc.dram_tensor("w2p_in", [HID * HSZ // 128, 64], I32, kind="ExternalInput").ap()
    s1x_in = nc.dram_tensor("s1x_in", [HSZ, HID], F32, kind="ExternalInput").ap()
    s3x_in = nc.dram_tensor("s3x_in", [HSZ, HID], F32, kind="ExternalInput").ap()
    s2x_in = nc.dram_tensor("s2x_in", [HID, HSZ], F32, kind="ExternalInput").ap()
    out_x = nc.dram_tensor("out_x", [NGRP * 128, HID], F32, kind="ExternalOutput").ap()

    xbf = nc.dram_tensor("xbf", [T_ALL, HID], BF16).ap()
    ipbf = nc.dram_tensor("ipbf", [NFT * 128, HID], BF16).ap()
    ipk = nc.dram_tensor("ipk", [NFT, HID, 128], BF16).ap()
    wobf = nc.dram_tensor("wobf", [HID, 1024], BF16).ap()
    w1u = nc.dram_tensor("w1u", [HSZ, HID], BF16).ap()
    w3u = nc.dram_tensor("w3u", [HSZ, HID], BF16).ap()
    w2u = nc.dram_tensor("w2u", [HID, HSZ], BF16).ap()
    ffn1 = nc.dram_tensor("ffn1", [NHT, HID, 128], BF16).ap()
    ffn3 = nc.dram_tensor("ffn3", [NHT, HID, 128], BF16).ap()
    ffn2 = nc.dram_tensor("ffn2", [32, HSZ, 128], BF16).ap()
    gatef = nc.dram_tensor("gatef", [NGRP, 1024, TG], BF16).ap()
    ydram = nc.dram_tensor("ydram", [NGRP, TG, 1024], BF16).ap()
    acrow_d = nc.dram_tensor("acrow_d", [1, 4096], F32).ap()
    rs1i = nc.dram_tensor("rs1i", [NGRP, NC * HID, 128], BF16).ap()
    rs1o = nc.dram_tensor("rs1o", [NGRP, HID, 128], BF16).ap()
    agi = nc.dram_tensor("agi", [NGRP, HID, 128], BF16).ap()
    ago = nc.dram_tensor("ago", [NGRP, NC * HID, 128], BF16).ap()
    x1tmd = nc.dram_tensor("x1tmd", [NGRP, 128, HID], BF16).ap()
    rs2i = nc.dram_tensor("rs2i", [NGRP, NC * HID, 128], BF16).ap()
    rs2o = nc.dram_tensor("rs2o", [NGRP, HID, 128], BF16).ap()
    ssmtap = nc.dram_tensor("ssmtap", [16, 128, 2048], F32).ap()

    rs1i3 = rs1i.rearrange("g (r d) t -> g r d t", r=NC)
    rs2i3 = rs2i.rearrange("g (r d) t -> g r d t", r=NC)
    ago3 = ago.rearrange("g (r d) t -> g r d t", r=NC)

    dram_map = dict(xbf=xbf, ipbf=ipbf, ipk=ipk, wobf=wobf, w1u=w1u, w3u=w3u,
                    w2u=w2u, ffn1=ffn1, ffn3=ffn3, ffn2=ffn2, gatef=gatef,
                    ydram=ydram, rs1i=rs1i, rs1o=rs1o, agi=agi, ago=ago,
                    x1tmd=x1tmd, rs2i=rs2i, rs2o=rs2o, acrow_d=acrow_d,
                    ssmtap=ssmtap)
    dbg_out = {}
    for name in debug:
        if name not in dram_map:
            continue
        src = dram_map[name]
        dbg_out[name] = (nc.dram_tensor("dbg_" + name, list(src.shape),
                                        src.dtype, kind="ExternalOutput").ap(), src)

    with B.es:
        aBF = B.sb("aBF", [128, 52224], BF16)
        aF = B.sb("aF", [128, 12296], F32)
        xT = aBF[:, 0:32768]
        wo_g = aBF[:, 0:32768]
        x1T = aBF[:, 0:32768]
        x_tm = aBF[:, 0:2048]
        xs_tm = aBF[:, 2048:4096]
        xsd_tm = aBF[:, 4096:6144]
        dres = aBF[:, 6144:8192]
        gate_tm = aBF[:, 8192:10240]
        ynorm = aBF[:, 10240:12288]
        b_tm = aBF[:, 12288:12544]
        gm_sb = aBF[:, 12544:13056]
        lt_sb = aBF[:, 13056:13568]
        gl_sb = [aBF[:, 13568:14080], aBF[:, 14080:14592]]
        ln_attn = aBF[:, 16384:20480]
        ln_bf = aBF[:, 20480:24576]
        xc32 = aBF[:, 24576:28672]
        wsl = [aBF[:, 32768:36864], aBF[:, 36864:40960]]
        yT = aBF[:, 32768:40960]
        rawx = aBF[:, 40960:51230]
        sa_h = aBF[:, 40960:52224]
        cvt = [aF[:, 0:1024], aF[:, 1024:2048]]
        yacc = aF[:, 2048:4096]
        acrow = aF[0:1, 4096:8192]
        dmat = aF[:, 8192:8704]
        rowt = aF[0:1, 8192:9216]
        dtraw = aF[:, 8960:9088]
        dtt1 = aF[:, 9088:9216]
        dtt2 = aF[:, 9216:9344]
        dtv = aF[:, 9344:9472]
        actm = aF[:, 9472:9600]
        acum = aF[:, 9600:9632]
        dec = aF[:, 9632:9664]
        sdec = aF[:, 9664:9696]
        dtdec = aF[:, 9696:9728]
        aclb = aF[:, 9728:9744]
        dlb = aF[:, 9744:9760]
        ssum = aF[:, 9760:9762]
        lnv = aF[:, 9762:9764]
        rstd = aF[:, 9764:9766]
        aclast = aF[0:1, 9768:9784]
        acr16 = aF[0:16, 9856:10112]
        zeros2 = aF[:, 10112:10368]
        cvw = aF[:, 10368:10408]
        cvb = aF[:, 10408:10418]
        dtb_b = aF[:, 10432:10560]
        nega_b = aF[:, 10560:10688]
        onesM = aF[:, 10688:10816]
        drep_b = aF[:, 10816:11840]
        ln_x = aF[:, 0:4096]
        ln_t = aF[:, 4096:8192]
        ln_sq = aF[:, 8192:12288]
        ln_o = aF[:, 8192:12288]
        st_s = aF[:, 12288:12296]
        tTL = B.sb("tTL", [128, 4096], BF16)
        unp = B.sb("unp", [128, 4096], BF16)
        sxp = B.sb("sxp", [128, 4096], BF16)
        pk0 = B.sb("pk0", [128, 4096], I32)
        ev1 = [B.sb(f"ev1_{i}", [128, 512], BF16) for i in range(4)]
        S_f = B.sb("S_f", [128, 1024], F32)
        S_bf = B.sb("S_bf", [128, 1024], BF16)
        stash = B.sb("stash", [128, 32], BF16)
        ident = B.sb("ident", [128, 128], BF16)
        ident32 = B.sb("ident32", [128, 128], F32)
        triu32 = B.sb("triu32", [128, 128], F32)
        ones32 = B.sb("ones32", [1, 128], F32)
        onescol = B.sb("onescol", [128, 1], F32)
        nw_b = B.sb("nw_b", [128, 1024], F32)

        pY = [B.ps(f"pY{i}", [128, 512], F32) for i in range(4)]
        pG0 = B.ps("pG0", [128, 512], F32)
        pG1 = B.ps("pG1", [128, 512], F32)
        pT = B.ps("pT", [128, 1024], BF16)
        pS = B.ps("pS", [128, 512], F32)

        for s in ["dS", "dG", "pe", "dv", "ac", "pl", "cc1", "cc2", "cc3"]:
            B.sem(s)
        block = B.es.enter_context(nc.Block())

        @block.gpsimd
        def _(g):
            g.memset(ident[:, :], 1.0)
            g.affine_select(out=ident[:, :], in_=ident[:, :], compare_op=ALU.is_ge,
                            fill=0.0, base=0, pattern=[[-1, 128]], channel_multiplier=1)
            g.affine_select(out=ident[:, :], in_=ident[:, :], compare_op=ALU.is_ge,
                            fill=0.0, base=0, pattern=[[1, 128]], channel_multiplier=-1)
            g.memset(ident32[:, :], 1.0)
            g.affine_select(out=ident32[:, :], in_=ident32[:, :], compare_op=ALU.is_ge,
                            fill=0.0, base=0, pattern=[[-1, 128]], channel_multiplier=1)
            g.affine_select(out=ident32[:, :], in_=ident32[:, :], compare_op=ALU.is_ge,
                            fill=0.0, base=0, pattern=[[1, 128]], channel_multiplier=-1)
            g.memset(triu32[:, :], 1.0)
            g.affine_select(out=triu32[:, :], in_=triu32[:, :], compare_op=ALU.is_ge,
                            fill=0.0, base=0, pattern=[[1, 128]], channel_multiplier=-1)
            g.memset(onesM[:, :], 1.0)
            g.memset(ones32[:, :], 1.0)
            g.memset(onescol[:, :], 1.0)
            B.inc(g.memset(zeros2[:, :], 0.0), "pl")
        consts_pl = B.cnt["pl"]

        @block.vector
        def _(v):
            v.memset(S_f[:, :], 0.0)
            B.inc(v.memset(S_bf[:, :], 0.0), "dv")
        s_dv = B.cnt["dv"]

        @block.gpsimd
        def _(g):
            B.inc(g.dma_start(out=ipbf, in_=wip_in), "dG", 16)
            B.inc(g.dma_start(out=wobf, in_=wo_in), "dG", 16)
            for gg in range(NGRP):
                B.inc(g.dma_start(
                    out=xbf.rearrange("(g t) k -> g t k", g=NGRP)[gg],
                    in_=x_in.rearrange("(g t) k -> g t k", g=NGRP)[gg]), "dG", 16)
            B.inc(g.dma_start(
                out=cvw[:, :].rearrange("p (f k) -> p f k", k=KCONV),
                in_=cvw_in.rearrange("(f p) k -> p f k", p=128)), "dG", 16)
            B.inc(g.dma_start(
                out=cvb[:, :],
                in_=cvb_in.rearrange("(f p) o -> p (f o)", p=128),
                allow_slow_non_contiguous=True), "dG", 16)
        cast_done = B.cnt["dG"]

        def bcast_const(src_ap, width, dst, postexp_neg=False):
            @block.gpsimd
            def _(g):
                B.wait(g, "dv")
                B.inc(g.dma_start(out=rowt[0:1, 0:width], in_=src_ap), "dG", 16)
            if postexp_neg:
                @block.scalar
                def _(s):
                    B.wait(s, "dG")
                    B.inc(s.activation(rowt[0:1, 0:width], rowt[0:1, 0:width],
                                       AF.Exp), "ac")
                @block.vector
                def _(v):
                    B.wait(v, "ac")
                    B.inc(v.tensor_scalar(out=rowt[0:1, 0:width],
                                          in0=rowt[0:1, 0:width],
                                          scalar1=-1.0, scalar2=None,
                                          op0=ALU.mult), "dv")
            for i in range((width + 511) // 512):
                w = min(512, width - i * 512)
                pb = [pS, pG0][i % 2]
                @block.tensor
                def _(t, i=i, w=w, pb=pb):
                    B.wait(t, "dG")
                    B.wait(t, "dv")
                    B.wait(t, "ac")
                    B.wait(t, "pl", consts_pl)
                    B.inc(t.matmul(pb[:, 0:w], ones32[0:1, :],
                                   rowt[0:1, i * 512:i * 512 + w],
                                   start=True, stop=True), "pe")
                @block.vector
                def _(v, i=i, w=w, pb=pb):
                    B.wait(v, "pe")
                    B.inc(v.tensor_copy(dst[:, i * 512:i * 512 + w], pb[:, 0:w]), "dv")

        bcast_const(dtb_in, 128, dtb_b)
        bcast_const(alog_in, 128, nega_b, postexp_neg=True)
        bcast_const(drep_in, 1024, drep_b)
        bcast_const(nw_in, 1024, nw_b)
        consts_dv = B.cnt["dv"]

        @block.sync
        def _(sy):
            sy.wait_ge(B.sems["dG"], cast_done)
            for kc in range(32):
                B.inc(sy.dma_start(tTL[:, 0:NFT * 128],
                                   ipbf[0:NFT * 128, kc * 128:(kc + 1) * 128],
                                   transpose=True), "dS", 16)
                B.wait(sy, "dS")
                B.inc(sy.dma_start(
                    out=ipk.rearrange("f (kc p) c -> kc p f c", kc=32)[kc],
                    in_=tTL[:, 0:NFT * 128].rearrange("p (f c) -> p f c", c=128)),
                    "dS", 16)
                B.wait(sy, "dS")
        ipk_done = B.cnt["dS"]

        for wi, (wp, sx, wu) in enumerate([(w1p_in, s1x_in, w1u),
                                           (w3p_in, s3x_in, w3u),
                                           (w2p_in, s2x_in, w2u)]):
            ntile = (HID // 128) if wi == 2 else NHT
            width = HSZ if wi == 2 else HID
            npk = width // 2
            for ti in range(ntile):
                @block.gpsimd
                def _(g, ti=ti, wp=wp, sx=sx, npk=npk, ntile=ntile, width=width):
                    B.wait(g, "dv")
                    B.inc(g.dma_start(
                        out=pk0[:, 0:npk],
                        in_=wp.rearrange("(t p r) c -> t p (r c)", t=ntile, p=128)[ti]),
                        "dG", 16)
                    B.inc(g.dma_start(
                        out=sxp[:, 0:width],
                        in_=sx.rearrange("(t p) c -> t p c", p=128)[ti]), "dG", 16)
                @block.vector
                def _(v, npk=npk, width=width):
                    B.wait(v, "dG")
                    v.tensor_scalar(out=pk0[:, npk:2 * npk], in0=pk0[:, 0:npk],
                                    scalar1=4, scalar2=None, op0=ALU.arith_shift_right)
                    v.tensor_tensor(
                        out=unp[:, 0:width].rearrange("p (b c) -> p b c", c=128)[:, :, 0:64],
                        in0=pk0[:, npk:2 * npk].rearrange("p (b c) -> p b c", c=64),
                        in1=sxp[:, 0:width].rearrange("p (b c) -> p b c", c=128)[:, :, 0:64],
                        op=ALU.mult)
                    v.tensor_scalar(out=pk0[:, npk:2 * npk], in0=pk0[:, 0:npk],
                                    scalar1=15, scalar2=8, op0=ALU.bitwise_and,
                                    op1=ALU.bitwise_xor)
                    B.inc(v.scalar_tensor_tensor(
                        out=unp[:, 0:width].rearrange("p (b c) -> p b c", c=128)[:, :, 64:128],
                        in0=pk0[:, npk:2 * npk].rearrange("p (b c) -> p b c", c=64),
                        scalar=-8.0,
                        in1=sxp[:, 0:width].rearrange("p (b c) -> p b c", c=128)[:, :, 64:128],
                        op0=ALU.add, op1=ALU.mult), "dv")
                @block.gpsimd
                def _(g, ti=ti, wu=wu, width=width):
                    B.wait(g, "dv")
                    B.inc(g.dma_start(
                        out=wu.rearrange("(t p) c -> t p c", p=128)[ti],
                        in_=unp[:, 0:width]), "dG", 16)
        unp_done = B.cnt["dG"]

        @block.sync
        def _(sy):
            sy.wait_ge(B.sems["dG"], unp_done)
            for kc in range(32):
                B.inc(sy.dma_start(tTL[:, 0:HSZ], w1u[0:HSZ, kc * 128:(kc + 1) * 128],
                                   transpose=True), "dS", 16)
                B.wait(sy, "dS")
                B.inc(sy.dma_start(
                    out=ffn1.rearrange("f (kc p) c -> kc p f c", kc=32)[kc],
                    in_=tTL[:, 0:HSZ].rearrange("p (f c) -> p f c", c=128)), "dS", 16)
                B.wait(sy, "dS")
            for kc in range(32):
                B.inc(sy.dma_start(tTL[:, 0:HSZ], w3u[0:HSZ, kc * 128:(kc + 1) * 128],
                                   transpose=True), "dS", 16)
                B.wait(sy, "dS")
                B.inc(sy.dma_start(
                    out=ffn3.rearrange("f (kc p) c -> kc p f c", kc=32)[kc],
                    in_=tTL[:, 0:HSZ].rearrange("p (f c) -> p f c", c=128)), "dS", 16)
                B.wait(sy, "dS")
            for hc in range(NHT):
                B.inc(sy.dma_start(tTL[:, 0:HID], w2u[0:HID, hc * 128:(hc + 1) * 128],
                                   transpose=True), "dS", 16)
                B.wait(sy, "dS")
                B.inc(sy.dma_start(
                    out=ffn2.rearrange("d (hc p) c -> hc p d c", hc=NHT)[hc],
                    in_=tTL[:, 0:HID].rearrange("p (d c) -> p d c", c=128)), "dS", 16)
                B.wait(sy, "dS")
        ffn_scr_done = B.cnt["dS"]

        bank_free = {i: ("pl", 0) for i in range(4)}
        pg_free = {0: ("pl", 0), 1: ("pl", 0)}
        slab_pe = {0: 0, 1: 0}
        ev1_free = {i: 0 for i in range(4)}

        for g_i in range(ngroups):
            seq_start = (g_i % 4 == 0)

            @block.sync
            def _(sy, g_i=g_i):
                if g_i == 0:
                    sy.wait_ge(B.sems["dG"], cast_done)
                B.wait(sy, "pe")
                B.wait(sy, "dv")
                B.wait(sy, "dG")
                for kc in range(32):
                    B.inc(sy.dma_start(
                        xT[:, kc * TG:(kc + 1) * TG],
                        xbf[g_i * TG:(g_i + 1) * TG, kc * 128:(kc + 1) * 128],
                        transpose=True), "dS", 16)
            xT_done = B.cnt["dS"]

            @block.vector
            def _(v, seq_start=seq_start):
                for ft in range(NCONV_FT):
                    if seq_start:
                        inst = v.memset(rawx[:, ft * RAWW:ft * RAWW + 3], 0.0)
                    else:
                        inst = v.tensor_copy(rawx[:, ft * RAWW:ft * RAWW + 3],
                                             stash[:, ft * 3:ft * 3 + 3])
                if seq_start:
                    v.memset(S_f[:, :], 0.0)
                    inst = v.memset(S_bf[:, :], 0.0)
                B.inc(inst, "dv")
            halo_dv = B.cnt["dv"]
            if seq_start:
                s_dv = halo_dv

            slab_ld = {}
            for ft in range(2):
                @block.sync
                def _(sy, ft=ft, g_i=g_i):
                    if g_i == 0 and ft == 0:
                        sy.wait_ge(B.sems["dS"], ipk_done)
                    sy.wait_ge(B.sems["pe"], slab_pe[ft % 2])
                    B.inc(sy.dma_start(
                        wsl[ft % 2][:, :].rearrange("p (kc c) -> p kc c", c=128),
                        ipk[ft].rearrange("(kc p) c -> p kc c", p=128)), "dS", 16)
                slab_ld[ft] = B.cnt["dS"]

            for ft in range(18):
                for t5 in range(2):
                    bank = (ft * 2 + t5) % 4
                    sem_n, sem_v = bank_free[bank]
                    @block.tensor
                    def _(t, ft=ft, t5=t5, bank=bank, sem_n=sem_n, sem_v=sem_v,
                          xT_done=xT_done):
                        t.wait_ge(B.sems["dS"], max(slab_ld[ft], xT_done))
                        t.wait_ge(B.sems[sem_n], sem_v)
                        for kc in range(32):
                            inst = t.matmul(
                                pY[bank][:, :],
                                wsl[ft % 2][:, kc * 128:(kc + 1) * 128],
                                xT[:, kc * TG + t5 * 512:kc * TG + t5 * 512 + 512],
                                start=(kc == 0), stop=(kc == 31))
                        B.inc(inst, "pe")
                    pe_c = B.cnt["pe"]
                    if t5 == 1:
                        slab_pe[ft % 2] = pe_c
                        if ft + 2 < 18:
                            @block.sync
                            def _(sy, ft=ft):
                                sy.wait_ge(B.sems["pe"], slab_pe[ft % 2])
                                B.inc(sy.dma_start(
                                    wsl[ft % 2][:, :].rearrange("p (kc c) -> p kc c", c=128),
                                    ipk[ft + 2].rearrange("(kc p) c -> p kc c", p=128)),
                                    "dS", 16)
                            slab_ld[ft + 2] = B.cnt["dS"]
                    if ft < NCONV_FT:
                        @block.vector
                        def _(v, ft=ft, t5=t5, bank=bank, pe_c=pe_c, halo_dv=halo_dv):
                            B.wait(v, "pe", pe_c)
                            B.wait(v, "dv", halo_dv)
                            B.inc(v.tensor_copy(
                                rawx[:, ft * RAWW + 3 + t5 * 512:
                                     ft * RAWW + 3 + t5 * 512 + 512],
                                pY[bank][:, :]), "dv")
                        bank_free[bank] = ("dv", B.cnt["dv"])
                    else:
                        @block.scalar
                        def _(s, bank=bank, pe_c=pe_c, evf=ev1_free[bank]):
                            B.wait(s, "pe", pe_c)
                            s.wait_ge(B.sems["dG"], evf)
                            B.inc(s.activation(ev1[bank][:, :], pY[bank][:, :],
                                               AF.Silu), "ac")
                        bank_free[bank] = ("ac", B.cnt["ac"])
                        ac_c = B.cnt["ac"]
                        @block.gpsimd
                        def _(g, ft=ft, t5=t5, bank=bank, g_i=g_i, ac_c=ac_c):
                            B.wait(g, "ac", ac_c)
                            B.inc(g.dma_start(
                                out=gatef[g_i][(ft - FT_G0) * 128:(ft - FT_G0 + 1) * 128,
                                               t5 * 512:t5 * 512 + 512],
                                in_=ev1[bank][:, :]), "dG", 16)
                        ev1_free[bank] = B.cnt["dG"]
            gate_stores = B.cnt["dG"]

            @block.sync
            def _(sy):
                sy.wait_ge(B.sems["pe"], slab_pe[0])
                B.inc(sy.dma_start(
                    wsl[0][:, :].rearrange("p (kc c) -> p kc c", c=128),
                    ipk[FT_DT].rearrange("(kc p) c -> p kc c", p=128)), "dS", 16)
            dt_slab = B.cnt["dS"]
            for tt in range(8):
                bankp = [pG0, pG1][tt % 2]
                sem_n, sem_v = pg_free[tt % 2]
                @block.tensor
                def _(t, tt=tt, bankp=bankp, sem_n=sem_n, sem_v=sem_v,
                      dt_slab=dt_slab, xT_done=xT_done):
                    t.wait_ge(B.sems["dS"], max(dt_slab, xT_done))
                    t.wait_ge(B.sems[sem_n], sem_v)
                    for kc in range(32):
                        inst = t.matmul(
                            bankp[:, 0:128],
                            xT[:, kc * TG + tt * 128:kc * TG + (tt + 1) * 128],
                            wsl[0][:, kc * 128:(kc + 1) * 128],
                            start=(kc == 0), stop=(kc == 31))
                    B.inc(inst, "pe")
                pe_c = B.cnt["pe"]
                @block.vector
                def _(v, tt=tt, bankp=bankp, pe_c=pe_c):
                    B.wait(v, "pe", pe_c)
                    B.inc(v.tensor_copy(dtraw[:, tt * 16:(tt + 1) * 16],
                                        bankp[:, 0:16]), "dv")
                pg_free[tt % 2] = ("dv", B.cnt["dv"])
            slab_pe[0] = B.cnt["pe"]
            dt_pe = B.cnt["pe"]

            @block.vector
            def _(v):
                B.wait(v, "dv", consts_dv)
                v.drain()
                B.inc(v.tensor_tensor(out=dtt1[:, :], in0=dtraw[:, :],
                                      in1=dtb_b[:, :], op=ALU.add), "dv")
            dv_c = B.cnt["dv"]
            @block.scalar
            def _(s, dv_c=dv_c):
                B.wait(s, "dv", dv_c)
                s.activation(dtt2[:, :], dtt1[:, :], AF.Abs)
                s.drain()
                s.activation(dtt2[:, :], dtt2[:, :], AF.Exp, scale=-1.0)
                s.drain()
                B.inc(s.activation(dtt2[:, :], dtt2[:, :], AF.Ln,
                                   bias=np.float32(1.0)), "ac")
            ac_c = B.cnt["ac"]
            @block.vector
            def _(v, ac_c=ac_c):
                B.wait(v, "ac", ac_c)
                v.tensor_scalar(out=dtv[:, :], in0=dtt1[:, :], scalar1=0.0,
                                scalar2=None, op0=ALU.max)
                v.drain()
                v.tensor_tensor(out=dtv[:, :], in0=dtv[:, :], in1=dtt2[:, :],
                                op=ALU.add)
                v.drain()
                B.inc(v.tensor_tensor(out=actm[:, :], in0=dtv[:, :], in1=nega_b[:, :],
                                      op=ALU.mult), "dv")
            dt_ready = B.cnt["dv"]

            for ft in range(NCONV_FT):
                buf = cvt[ft % 2]
                @block.vector
                def _(v, ft=ft, buf=buf):
                    if ft >= 2:
                        B.wait(v, "ac")
                    off = ft * RAWW
                    v.tensor_copy(stash[:, ft * 3:ft * 3 + 3],
                                  rawx[:, off + TG:off + TG + 3])
                    v.tensor_scalar(out=buf[:, :], in0=rawx[:, off:off + TG],
                                    scalar1=cvw[:, ft * KCONV:ft * KCONV + 1],
                                    scalar2=None, op0=ALU.mult)
                    for j in range(1, KCONV):
                        inst = v.scalar_tensor_tensor(
                            out=buf[:, :], in0=rawx[:, off + j:off + j + TG],
                            scalar=cvw[:, ft * KCONV + j:ft * KCONV + j + 1],
                            in1=buf[:, :], op0=ALU.mult, op1=ALU.add)
                    B.inc(inst, "dv")
                dv_c2 = B.cnt["dv"]
                @block.scalar
                def _(s, ft=ft, buf=buf, dv_c2=dv_c2):
                    B.wait(s, "dv", dv_c2)
                    B.inc(s.activation(rawx[:, ft * RAWW + 3:ft * RAWW + 3 + TG],
                                       buf[:, :], AF.Silu, bias=cvb[:, ft:ft + 1]),
                          "ac")
            conv_ac = B.cnt["ac"]

            def XC(ft, a, b):
                return rawx[:, ft * RAWW + 3 + a:ft * RAWW + 3 + b]

            for ci in range(nchunks):
                q0, q1 = 2 * ci, 2 * ci + 1
                c_off = ci * CS

                @block.sync
                def _(sy, g_i=g_i, ci=ci, gate_stores=gate_stores, dt_pe=dt_pe):
                    sy.wait_ge(B.sems["dG"], gate_stores)
                    sy.wait_ge(B.sems["pe"], dt_pe)
                    B.wait(sy, "dv")
                    for q in range(2):
                        B.inc(sy.dma_start(
                            gate_tm[:, q * 1024:(q + 1) * 1024],
                            gatef[g_i][0:1024,
                                       ci * CS + q * 128:ci * CS + (q + 1) * 128],
                            transpose=True), "dS", 16)
                gate_tl = B.cnt["dS"]

                @block.tensor
                def _(t, q0=q0, q1=q1, dt_ready=dt_ready):
                    B.wait(t, "dv", dt_ready)
                    B.wait(t, "dv")
                    t.matmul(pS[:, 0:16], triu32[:, :], actm[:, q0 * 16:q0 * 16 + 16],
                             start=True, stop=True)
                    t.matmul(pS[:, 16:32], onesM[:, :], actm[:, q0 * 16:q0 * 16 + 16],
                             start=True, stop=False)
                    t.matmul(pS[:, 16:32], triu32[:, :], actm[:, q1 * 16:q1 * 16 + 16],
                             start=False, stop=True)
                    t.matmul(pS[0:1, 32:48], onescol[:, 0:1],
                             actm[:, q0 * 16:q0 * 16 + 16], start=True, stop=False)
                    B.inc(t.matmul(pS[0:1, 32:48], onescol[:, 0:1],
                                   actm[:, q1 * 16:q1 * 16 + 16],
                                   start=False, stop=True), "pe")
                pe_ac2 = B.cnt["pe"]
                @block.vector
                def _(v, pe_ac2=pe_ac2):
                    B.wait(v, "pe", pe_ac2)
                    v.tensor_copy(acum[:, :], pS[:, 0:32])
                    B.inc(v.tensor_copy(aclast[0:1, :], pS[0:1, 32:48]), "dv")
                ac_ev = B.cnt["dv"]

                @block.tensor
                def _(t, ac_ev=ac_ev):
                    B.wait(t, "dv", ac_ev)
                    t.transpose(pS[0:16, 256:384], acum[:, 0:16], ident32[:, :])
                    B.inc(t.transpose(pS[0:16, 384:512], acum[:, 16:32],
                                      ident32[:, :]), "pe")
                pe_tr = B.cnt["pe"]
                @block.vector
                def _(v, pe_tr=pe_tr):
                    B.wait(v, "pe", pe_tr)
                    v.tensor_copy(acr16[0:16, 0:128], pS[0:16, 256:384])
                    B.inc(v.tensor_copy(acr16[0:16, 128:256], pS[0:16, 384:512]), "dv")
                acr_ev = B.cnt["dv"]
                @block.gpsimd
                def _(g, acr_ev=acr_ev):
                    B.wait(g, "dv", acr_ev)
                    B.wait(g, "dG")
                    B.inc(g.dma_start(
                        out=acrow_d.rearrange("o (h t) -> (o h) t", t=256),
                        in_=acr16[0:16, :]), "dG", 16)
                    B.wait(g, "dG")
                    B.inc(g.dma_start(out=acrow[0:1, :], in_=acrow_d), "dG", 16)
                acrow_ld = B.cnt["dG"]

                @block.tensor
                def _(t):
                    B.wait(t, "dv")
                    B.inc(t.matmul(pS[:, 0:16], ones32[0:1, :], aclast[0:1, :],
                                   start=True, stop=True), "pe")
                pe_ab = B.cnt["pe"]
                @block.vector
                def _(v, pe_ab=pe_ab):
                    B.wait(v, "pe", pe_ab)
                    v.tensor_copy(aclb[:, :], pS[:, 0:16])
                    v.drain()
                    v.tensor_tensor(out=dec[:, 0:16], in0=aclb[:, :],
                                    in1=acum[:, 0:16], op=ALU.subtract)
                    B.inc(v.tensor_tensor(out=dec[:, 16:32], in0=aclb[:, :],
                                          in1=acum[:, 16:32], op=ALU.subtract), "dv")
                dec_dv = B.cnt["dv"]
                @block.scalar
                def _(s, dec_dv=dec_dv):
                    B.wait(s, "dv", dec_dv)
                    s.activation(dec[:, :], dec[:, :], AF.Exp)
                    s.activation(sdec[:, :], acum[:, :], AF.Exp)
                    B.inc(s.activation(dlb[:, :], aclb[:, :], AF.Exp), "ac")
                dec_ac = B.cnt["ac"]
                @block.vector
                def _(v, q0=q0, q1=q1, dec_ac=dec_ac):
                    B.wait(v, "ac", dec_ac)
                    v.tensor_tensor(out=dtdec[:, 0:16],
                                    in0=dtv[:, q0 * 16:q0 * 16 + 16],
                                    in1=dec[:, 0:16], op=ALU.mult)
                    B.inc(v.tensor_tensor(out=dtdec[:, 16:32],
                                          in0=dtv[:, q1 * 16:q1 * 16 + 16],
                                          in1=dec[:, 16:32], op=ALU.mult), "dv")
                dtdec_dv = B.cnt["dv"]

                for q in range(2):
                    for hh in range(2):
                        bank = q * 2 + hh
                        sem_n, sem_v = bank_free[bank]
                        @block.tensor
                        def _(t, q=q, hh=hh, bank=bank, sem_n=sem_n, sem_v=sem_v,
                              c_off=c_off, conv_ac=conv_ac, s_prev=s_dv):
                            B.wait(t, "ac", conv_ac)
                            B.wait(t, "dv", s_prev)
                            t.wait_ge(B.sems[sem_n], sem_v)
                            B.inc(t.matmul(
                                pY[bank][:, :],
                                XC(FT_C, c_off + q * 128, c_off + (q + 1) * 128),
                                S_bf[:, hh * 512:(hh + 1) * 512],
                                start=True, stop=True), "pe")
                yoff_pe = B.cnt["pe"]

                for q in range(2):
                    @block.tensor
                    def _(t, q=q, c_off=c_off, conv_ac=conv_ac):
                        B.wait(t, "ac", conv_ac)
                        B.wait(t, "dv")
                        for ft in range(8):
                            inst = t.transpose(
                                pT[:, ft * 128:(ft + 1) * 128],
                                XC(ft, c_off + q * 128, c_off + (q + 1) * 128),
                                ident[:, :])
                        B.inc(inst, "pe")
                    pe_x = B.cnt["pe"]
                    @block.vector
                    def _(v, q=q, pe_x=pe_x):
                        B.wait(v, "pe", pe_x)
                        B.inc(v.tensor_copy(x_tm[:, q * 1024:(q + 1) * 1024],
                                            pT[:, :]), "dv")
                @block.tensor
                def _(t, c_off=c_off):
                    B.wait(t, "dv")
                    for q in range(2):
                        inst = t.transpose(
                            pT[:, q * 128:(q + 1) * 128],
                            XC(FT_B, c_off + q * 128, c_off + (q + 1) * 128),
                            ident[:, :])
                    B.inc(inst, "pe")
                pe_b = B.cnt["pe"]
                @block.vector
                def _(v, pe_b=pe_b):
                    B.wait(v, "pe", pe_b)
                    B.inc(v.tensor_copy(b_tm[:, :], pT[:, 0:256]), "dv")
                btm_dv = B.cnt["dv"]

                @block.vector
                def _(v, q0=q0, q1=q1, dtdec_dv=dtdec_dv):
                    B.wait(v, "dv", dtdec_dv)
                    for q, qq in [(0, q0), (1, q1)]:
                        for h in range(16):
                            v.tensor_scalar(
                                out=xs_tm[:, q * 1024 + h * 64:q * 1024 + (h + 1) * 64],
                                in0=x_tm[:, q * 1024 + h * 64:q * 1024 + (h + 1) * 64],
                                scalar1=dtv[:, qq * 16 + h:qq * 16 + h + 1],
                                scalar2=None, op0=ALU.mult)
                            v.tensor_scalar(
                                out=xsd_tm[:, q * 1024 + h * 64:q * 1024 + (h + 1) * 64],
                                in0=x_tm[:, q * 1024 + h * 64:q * 1024 + (h + 1) * 64],
                                scalar1=dtdec[:, q * 16 + h:q * 16 + h + 1],
                                scalar2=None, op0=ALU.mult)
                    for q in range(2):
                        inst = v.tensor_tensor(
                            out=dres[:, q * 1024:(q + 1) * 1024],
                            in0=x_tm[:, q * 1024:(q + 1) * 1024],
                            in1=drep_b[:, :], op=ALU.mult)
                    B.inc(inst, "dv")
                xs_dv = B.cnt["dv"]

                sem0, sv0 = pg_free[0]
                sem1, sv1 = pg_free[1]
                @block.tensor
                def _(t, c_off=c_off, conv_ac=conv_ac, sem0=sem0, sv0=sv0,
                      sem1=sem1, sv1=sv1):
                    B.wait(t, "ac", conv_ac)
                    t.wait_ge(B.sems[sem0], sv0)
                    t.wait_ge(B.sems[sem1], sv1)
                    t.matmul(pG0[:, 0:256],
                             XC(FT_B, c_off, c_off + 128),
                             XC(FT_C, c_off, c_off + 256),
                             start=True, stop=True)
                    B.inc(t.matmul(pG1[:, 0:256],
                                   XC(FT_B, c_off + 128, c_off + 256),
                                   XC(FT_C, c_off, c_off + 256),
                                   start=True, stop=True), "pe")
                gm_pe = B.cnt["pe"]
                @block.vector
                def _(v, gm_pe=gm_pe):
                    B.wait(v, "pe", gm_pe)
                    v.tensor_copy(gm_sb[:, 0:256], pG0[:, 0:256])
                    B.inc(v.tensor_copy(gm_sb[:, 256:512], pG1[:, 0:256]), "dv")
                gm_dv = B.cnt["dv"]
                pg_free[0] = ("dv", gm_dv)
                pg_free[1] = ("dv", gm_dv)
                @block.gpsimd
                def _(g, gm_dv=gm_dv):
                    B.wait(g, "dv", gm_dv)
                    g.affine_select(out=gm_sb[:, 0:256], in_=gm_sb[:, 0:256],
                                    compare_op=ALU.is_ge, fill=0.0, base=0,
                                    pattern=[[1, 256]], channel_multiplier=-1)
                    B.inc(g.affine_select(out=gm_sb[:, 256:512], in_=gm_sb[:, 256:512],
                                          compare_op=ALU.is_ge, fill=0.0, base=-128,
                                          pattern=[[1, 256]], channel_multiplier=-1),
                          "pl")
                gm_pl = B.cnt["pl"]

                @block.vector
                def _(v, q0=q0, q1=q1, yoff_pe=yoff_pe, xs_dv=xs_dv, dec_ac=dec_ac):
                    B.wait(v, "pe", yoff_pe)
                    B.wait(v, "dv", xs_dv)
                    B.wait(v, "ac", dec_ac)
                    for q, qq in [(0, q0), (1, q1)]:
                        for h in range(16):
                            v.tensor_scalar(
                                out=yacc[:, q * 1024 + h * 64:q * 1024 + (h + 1) * 64],
                                in0=pY[q * 2 + h // 8][:, (h % 8) * 64:(h % 8 + 1) * 64],
                                scalar1=sdec[:, q * 16 + h:q * 16 + h + 1],
                                scalar2=None, op0=ALU.mult)
                    for q in range(2):
                        inst = v.tensor_tensor(
                            out=yacc[:, q * 1024:(q + 1) * 1024],
                            in0=yacc[:, q * 1024:(q + 1) * 1024],
                            in1=dres[:, q * 1024:(q + 1) * 1024], op=ALU.add)
                    B.inc(inst, "dv")
                yoff_ev = B.cnt["dv"]
                for bk in range(4):
                    bank_free[bk] = ("dv", yoff_ev)

                gl_pe = {0: 0, 1: 0}
                bc_pe = 0
                for h in range(16):
                    if h % 2 == 0:
                        @block.tensor
                        def _(t, h=h, acrow_ld=acrow_ld):
                            t.wait_ge(B.sems["dG"], acrow_ld)
                            B.wait(t, "dv")
                            B.inc(t.matmul(
                                pS[:, :], ones32[0:1, :],
                                acrow[0:1, (h // 2) * 512:(h // 2) * 512 + 512],
                                start=True, stop=True), "pe")
                        bc_pe = B.cnt["pe"]
                    @block.vector
                    def _(v, h=h, bc_pe=bc_pe):
                        B.wait(v, "pe", bc_pe)
                        po = (h % 2) * 256
                        for jt in range(2):
                            inst = v.scalar_tensor_tensor(
                                out=dmat[:, jt * 256:(jt + 1) * 256],
                                in0=pS[:, po:po + 256],
                                scalar=acum[:, jt * 16 + h:jt * 16 + h + 1],
                                in1=zeros2[:, :], op0=ALU.subtract, op1=ALU.min)
                        B.inc(inst, "dv")
                    stt_dv = B.cnt["dv"]
                    @block.scalar
                    def _(s, stt_dv=stt_dv):
                        B.wait(s, "dv", stt_dv)
                        B.inc(s.activation(lt_sb[:, :], dmat[:, :], AF.Exp), "ac")
                    lt_ac = B.cnt["ac"]
                    @block.vector
                    def _(v, h=h, lt_ac=lt_ac, gm_pl=gm_pl):
                        B.wait(v, "ac", lt_ac)
                        B.wait(v, "pl", gm_pl)
                        v.wait_ge(B.sems["pe"], gl_pe[h % 2])
                        B.inc(v.tensor_tensor(out=gl_sb[h % 2][:, :], in0=lt_sb[:, :],
                                              in1=gm_sb[:, :], op=ALU.mult), "dv")
                    gl_dv = B.cnt["dv"]
                    @block.tensor
                    def _(t, h=h, gl_dv=gl_dv):
                        B.wait(t, "dv", gl_dv)
                        glb = gl_sb[h % 2]
                        hh = h // 8
                        col = (h % 8) * 64
                        t.matmul(pY[hh][:, col:col + 64],
                                 glb[:, 0:128], xs_tm[:, h * 64:h * 64 + 64],
                                 start=True, stop=True)
                        t.matmul(pY[2 + hh][:, col:col + 64],
                                 glb[:, 128:256], xs_tm[:, h * 64:h * 64 + 64],
                                 start=True, stop=False)
                        B.inc(t.matmul(pY[2 + hh][:, col:col + 64],
                                       glb[:, 384:512],
                                       xs_tm[:, 1024 + h * 64:1024 + h * 64 + 64],
                                       start=False, stop=True), "pe")
                    gl_pe[h % 2] = B.cnt["pe"]
                yd_pe = B.cnt["pe"]

                sem0, sv0 = pg_free[0]
                sem1, sv1 = pg_free[1]
                @block.tensor
                def _(t, btm_dv=btm_dv, xs_dv=xs_dv, sem0=sem0, sv0=sv0,
                      sem1=sem1, sv1=sv1):
                    B.wait(t, "dv", max(btm_dv, xs_dv))
                    t.wait_ge(B.sems[sem0], sv0)
                    t.wait_ge(B.sems[sem1], sv1)
                    for hh, pg in [(0, pG0), (1, pG1)]:
                        for jt in range(2):
                            inst = t.matmul(
                                pg[:, :], b_tm[:, jt * 128:(jt + 1) * 128],
                                xsd_tm[:, jt * 1024 + hh * 512:
                                       jt * 1024 + (hh + 1) * 512],
                                start=(jt == 0), stop=(jt == 1))
                    B.inc(inst, "pe")
                st_pe = B.cnt["pe"]

                @block.vector
                def _(v, yd_pe=yd_pe):
                    B.wait(v, "pe", yd_pe)
                    for q in range(2):
                        for hh in range(2):
                            inst = v.tensor_tensor(
                                out=yacc[:, q * 1024 + hh * 512:
                                         q * 1024 + (hh + 1) * 512],
                                in0=yacc[:, q * 1024 + hh * 512:
                                         q * 1024 + (hh + 1) * 512],
                                in1=pY[q * 2 + hh][:, :], op=ALU.add)
                    B.inc(inst, "dv")
                y_dv = B.cnt["dv"]
                for bk in range(4):
                    bank_free[bk] = ("dv", y_dv)

                @block.vector
                def _(v, st_pe=st_pe, dec_ac=dec_ac):
                    B.wait(v, "pe", st_pe)
                    B.wait(v, "ac", dec_ac)
                    for hh, pg in [(0, pG0), (1, pG1)]:
                        for h8 in range(8):
                            h = hh * 8 + h8
                            v.scalar_tensor_tensor(
                                out=S_f[:, h * 64:(h + 1) * 64],
                                in0=S_f[:, h * 64:(h + 1) * 64],
                                scalar=dlb[:, h:h + 1],
                                in1=pg[:, h8 * 64:(h8 + 1) * 64],
                                op0=ALU.mult, op1=ALU.add)
                    v.drain()
                    v.tensor_copy(S_bf[:, 0:512], S_f[:, 0:512])
                    B.inc(v.tensor_copy(S_bf[:, 512:1024], S_f[:, 512:1024]), "dv")
                s_dv = B.cnt["dv"]
                pg_free[0] = ("dv", s_dv)
                pg_free[1] = ("dv", s_dv)

                @block.vector
                def _(v, gate_tl=gate_tl, y_dv=y_dv):
                    v.wait_ge(B.sems["dS"], gate_tl)
                    B.wait(v, "dv", y_dv)
                    for q in range(2):
                        v.tensor_tensor(out=yacc[:, q * 1024:(q + 1) * 1024],
                                        in0=yacc[:, q * 1024:(q + 1) * 1024],
                                        in1=gate_tm[:, q * 1024:(q + 1) * 1024],
                                        op=ALU.mult)
                    v.drain()
                    for q in range(2):
                        inst = v.scalar_tensor_tensor(
                            out=dres[:, q * 1024:(q + 1) * 1024],
                            in0=yacc[:, q * 1024:(q + 1) * 1024], scalar=1.0,
                            in1=yacc[:, q * 1024:(q + 1) * 1024],
                            op0=ALU.mult, op1=ALU.mult,
                            accum_out=ssum[:, q:q + 1])
                    B.inc(inst, "dv")
                rms_dv = B.cnt["dv"]
                if debug and g_i == 0 and ci == 0:
                    @block.gpsimd
                    def _(g, rms_dv=rms_dv):
                        B.wait(g, "dv", rms_dv)
                        for ti, (src_ap, wdt) in enumerate([
                                (x_tm, 2048), (xs_tm, 2048), (xsd_tm, 2048),
                                (dres, 2048), (b_tm, 256), (gm_sb, 512),
                                (gl_sb[1], 512), (acum, 32), (sdec, 32),
                                (dec, 32), (dtv, 128), (actm, 128),
                                (yacc, 2048), (gate_tm, 2048)]):
                            B.inc(g.dma_start(out=ssmtap[ti][:, 0:wdt],
                                              in_=src_ap), "dG", 16)
                @block.scalar
                def _(s, rms_dv=rms_dv):
                    B.wait(s, "dv", rms_dv)
                    B.inc(s.activation(lnv[:, :], ssum[:, :], AF.Ln,
                                       scale=np.float32(1.0 / 1024.0), bias=EPS), "ac")
                rms_ac = B.cnt["ac"]
                @block.vector
                def _(v, rms_ac=rms_ac):
                    B.wait(v, "ac", rms_ac)
                    B.inc(v.tensor_scalar(out=lnv[:, :], in0=lnv[:, :], scalar1=-0.5,
                                          scalar2=None, op0=ALU.mult), "dv")
                rms_dv2 = B.cnt["dv"]
                @block.scalar
                def _(s, rms_dv2=rms_dv2):
                    B.wait(s, "dv", rms_dv2)
                    B.inc(s.activation(rstd[:, :], lnv[:, :], AF.Exp), "ac")
                rms_ac2 = B.cnt["ac"]
                @block.vector
                def _(v, rms_ac2=rms_ac2):
                    B.wait(v, "ac", rms_ac2)
                    for q in range(2):
                        inst = v.scalar_tensor_tensor(
                            out=ynorm[:, q * 1024:(q + 1) * 1024],
                            in0=yacc[:, q * 1024:(q + 1) * 1024],
                            scalar=rstd[:, q:q + 1],
                            in1=nw_b[:, :], op0=ALU.mult, op1=ALU.mult)
                    B.inc(inst, "dv")
                yn_dv = B.cnt["dv"]
                @block.gpsimd
                def _(g, g_i=g_i, ci=ci, yn_dv=yn_dv):
                    B.wait(g, "dv", yn_dv)
                    for q in range(2):
                        B.inc(g.dma_start(
                            out=ydram[g_i][ci * CS + q * 128:ci * CS + (q + 1) * 128, :],
                            in_=ynorm[:, q * 1024:(q + 1) * 1024]), "dG", 16)
            y_stores = B.cnt["dG"]

            @block.sync
            def _(sy, g_i=g_i, y_stores=y_stores):
                sy.wait_ge(B.sems["dG"], y_stores)
                B.wait(sy, "pe")
                B.wait(sy, "dv")
                for fc in range(8):
                    B.inc(sy.dma_start(wo_g[:, fc * HID:(fc + 1) * HID],
                                       wobf[0:HID, fc * 128:(fc + 1) * 128],
                                       transpose=True), "dS", 16)
                for fc in range(8):
                    B.inc(sy.dma_start(
                        yT[:, fc * TG:(fc + 1) * TG],
                        ydram[g_i][0:TG, fc * 128:(fc + 1) * 128],
                        transpose=True), "dS", 16)
            yT_done = B.cnt["dS"]

            for dt_i in range(32):
                for t5 in range(2):
                    bank = (dt_i * 2 + t5) % 4
                    sem_n, sem_v = bank_free[bank]
                    @block.tensor
                    def _(t, dt_i=dt_i, t5=t5, bank=bank, sem_n=sem_n, sem_v=sem_v,
                          yT_done=yT_done):
                        t.wait_ge(B.sems["dS"], yT_done)
                        t.wait_ge(B.sems[sem_n], sem_v)
                        for fc in range(8):
                            inst = t.matmul(
                                pY[bank][:, :],
                                wo_g[:, fc * HID + dt_i * 128:
                                     fc * HID + (dt_i + 1) * 128],
                                yT[:, fc * TG + t5 * 512:fc * TG + t5 * 512 + 512],
                                start=(fc == 0), stop=(fc == 7))
                        B.inc(inst, "pe")
                    pe_c = B.cnt["pe"]
                    @block.vector
                    def _(v, bank=bank, pe_c=pe_c, evf=ev1_free[bank]):
                        B.wait(v, "pe", pe_c)
                        v.wait_ge(B.sems["dG"], evf)
                        B.inc(v.tensor_copy(ev1[bank][:, :], pY[bank][:, :]), "dv")
                    ev_dv = B.cnt["dv"]
                    bank_free[bank] = ("dv", ev_dv)
                    @block.gpsimd
                    def _(g, g_i=g_i, dt_i=dt_i, t5=t5, bank=bank, ev_dv=ev_dv):
                        B.wait(g, "dv", ev_dv)
                        B.inc(g.dma_start(
                            out=rs1i3[g_i][t5 * 4:(t5 + 1) * 4,
                                           dt_i * 128:(dt_i + 1) * 128, :]
                            .rearrange("r p t -> p r t"),
                            in_=ev1[bank][:, :].rearrange("p (r t) -> p r t", r=4)),
                            "dG", 16)
                    ev1_free[bank] = B.cnt["dG"]
            op_stores = B.cnt["dG"]

            @block.gpsimd
            def _(g, g_i=g_i, op_stores=op_stores):
                g.wait_ge(B.sems["dG"], op_stores)
                B.inc(g.collective_compute(
                    "ReduceScatter", ALU.add,
                    replica_groups=[list(range(NC))],
                    ins=[rs1i[g_i].opt()], outs=[rs1o[g_i].opt()]), "cc1")

        mixer_pe = B.cnt["pe"]

        for g_i in range(ngroups):
            @block.sync
            def _(sy, g_i=g_i, mixer_pe=mixer_pe):
                sy.wait_ge(B.sems["cc1"], ngroups)
                if g_i == 0:
                    sy.wait_ge(B.sems["pe"], mixer_pe)
                B.wait(sy, "dv")
                B.wait(sy, "dG")
                B.inc(sy.dma_start(ln_attn[:, :], rs1o[g_i][0:HID, 0:128],
                                   transpose=True), "dS", 16)
            at_ld = B.cnt["dS"]
            @block.gpsimd
            def _(g, g_i=g_i):
                B.wait(g, "dv")
                B.inc(g.dma_start(out=ln_x[:, :],
                                  in_=xrow_in[g_i * 128:(g_i + 1) * 128, :]), "dG", 16)
            x_ld = B.cnt["dG"]
            @block.vector
            def _(v, at_ld=at_ld, x_ld=x_ld, g_i=g_i, mixer_pe=mixer_pe):
                v.wait_ge(B.sems["dS"], at_ld)
                v.wait_ge(B.sems["dG"], x_ld)
                if g_i == 0:
                    v.wait_ge(B.sems["pe"], mixer_pe)
                v.scalar_tensor_tensor(out=ln_t[:, :], in0=ln_x[:, :], scalar=0.0,
                                       in1=ln_attn[:, :], op0=ALU.add, op1=ALU.add)
                v.drain()
                v.reduce_sum(st_s[:, 0:1], ln_t[:, :], axis=mybir.AxisListType.X)
                v.scalar_tensor_tensor(out=ln_sq[:, :], in0=ln_t[:, :], scalar=1.0,
                                       in1=ln_t[:, :], op0=ALU.mult, op1=ALU.mult,
                                       accum_out=st_s[:, 1:2])
                v.drain()
                v.tensor_scalar(out=st_s[:, 2:3], in0=st_s[:, 0:1],
                                scalar1=1.0 / HID, scalar2=None,
                                op0=ALU.mult)
                v.drain()
                v.tensor_tensor(out=st_s[:, 3:4], in0=st_s[:, 2:3],
                                in1=st_s[:, 2:3], op=ALU.mult)
                v.drain()
                B.inc(v.scalar_tensor_tensor(
                    out=st_s[:, 4:5], in0=st_s[:, 1:2],
                    scalar=1.0 / HID, in1=st_s[:, 3:4],
                    op0=ALU.mult, op1=ALU.subtract), "dv")
            var_dv = B.cnt["dv"]
            @block.scalar
            def _(s, var_dv=var_dv):
                B.wait(s, "dv", var_dv)
                B.inc(s.activation(st_s[:, 5:6], st_s[:, 4:5], AF.Ln, bias=EPS), "ac")
            ln_ac = B.cnt["ac"]
            @block.vector
            def _(v, ln_ac=ln_ac):
                B.wait(v, "ac", ln_ac)
                B.inc(v.tensor_scalar(out=st_s[:, 6:7], in0=st_s[:, 5:6],
                                      scalar1=-0.5, scalar2=None, op0=ALU.mult), "dv")
            ln_dv2 = B.cnt["dv"]
            @block.scalar
            def _(s, ln_dv2=ln_dv2):
                B.wait(s, "dv", ln_dv2)
                B.inc(s.activation(st_s[:, 7:8], st_s[:, 6:7], AF.Exp), "ac")
            ln_ac2 = B.cnt["ac"]
            if "lntap" in debug and g_i == 0:
                @block.gpsimd
                def _(g, m=B.cnt["ac"]):
                    B.wait(g, "ac", m)
                    B.inc(g.dma_start(out=ssmtap[0][:, 0:8], in_=st_s), "dG", 16)
                    B.inc(g.dma_start(out=ssmtap[1], in_=ln_t[:, 0:2048]), "dG", 16)
                    B.inc(g.dma_start(out=ssmtap[2], in_=ln_t[:, 2048:4096]), "dG", 16)
                @block.vector
                def _(v, m=B.cnt["dG"]):
                    v.wait_ge(B.sems["dG"], m)
            @block.vector
            def _(v, ln_ac2=ln_ac2):
                B.wait(v, "ac", ln_ac2)
                B.inc(v.tensor_scalar(out=ln_bf[:, :], in0=ln_t[:, :],
                                      scalar1=st_s[:, 2:3], scalar2=st_s[:, 7:8],
                                      op0=ALU.subtract, op1=ALU.mult), "dv")
            x1_dv = B.cnt["dv"]
            @block.gpsimd
            def _(g, g_i=g_i, x1_dv=x1_dv):
                B.wait(g, "dv", x1_dv)
                B.inc(g.dma_start(out=x1tmd[g_i], in_=ln_bf[:, :]), "dG", 16)
            x1_st = B.cnt["dG"]
            @block.sync
            def _(sy, g_i=g_i, x1_st=x1_st):
                sy.wait_ge(B.sems["dG"], x1_st)
                B.wait(sy, "dv")
                B.wait(sy, "dG")
                for dc in range(32):
                    B.inc(sy.dma_start(
                        xc32[:, dc * 128:(dc + 1) * 128],
                        x1tmd[g_i][0:128, dc * 128:(dc + 1) * 128],
                        transpose=True), "dS", 16)
            xc_ld = B.cnt["dS"]
            @block.gpsimd
            def _(g, g_i=g_i, xc_ld=xc_ld):
                g.wait_ge(B.sems["dS"], xc_ld)
                B.inc(g.dma_start(
                    out=agi[g_i].rearrange("(dc p) t -> p dc t", p=128),
                    in_=xc32[:, :].rearrange("p (dc t) -> p dc t", t=128)), "dG", 16)
                B.wait(g, "dG")
                B.inc(g.collective_compute(
                    "AllGather", ALU.bypass,
                    replica_groups=[list(range(NC))],
                    ins=[agi[g_i].opt()], outs=[ago[g_i].opt()]), "cc2")

        x1T_pe = B.cnt["pe"]
        ffn_dv0 = B.cnt["dv"]
        for g_i in range(ngroups):
            @block.sync
            def _(sy, g_i=g_i, x1T_pe=x1T_pe, ffn_dv0=ffn_dv0):
                sy.wait_ge(B.sems["cc2"], ngroups)
                sy.wait_ge(B.sems["pe"], x1T_pe)
                if g_i == 0:
                    sy.wait_ge(B.sems["dv"], ffn_dv0)
                for kc in range(32):
                    B.inc(sy.dma_start(
                        out=x1T[:, kc * TG:(kc + 1) * TG].rearrange(
                            "p (r t) -> p r t", r=NC),
                        in_=ago3[g_i][0:NC, kc * 128:(kc + 1) * 128, :]
                        .rearrange("r p t -> p r t")), "dS", 16)
            x1_done = B.cnt["dS"]

            ffn_phase_taps = ("ffntap" in debug and g_i == 0)
            for phase, scr in enumerate([ffn1, ffn3, ffn2]):
                nslab = NHT if phase < 2 else 32
                slw = 4096 if phase < 2 else NHT * 128
                nk = 32 if phase < 2 else NHT
                slab_ld2 = {}
                for sl in range(min(2, nslab)):
                    @block.sync
                    def _(sy, sl=sl, scr=scr, slw=slw, phase=phase, g_i=g_i):
                        if g_i == 0 and phase == 0 and sl == 0:
                            sy.wait_ge(B.sems["dS"], ffn_scr_done)
                        sy.wait_ge(B.sems["pe"], slab_pe[sl % 2])
                        B.inc(sy.dma_start(
                            wsl[sl % 2][:, 0:slw].rearrange("p (kc c) -> p kc c", c=128),
                            scr[sl].rearrange("(kc p) c -> p kc c", p=128)),
                            "dS", 16)
                    slab_ld2[sl] = B.cnt["dS"]
                for sl in range(nslab):
                    for t5 in range(2):
                        bank = (sl * 2 + t5) % 4
                        sem_n, sem_v = bank_free[bank]
                        @block.tensor
                        def _(t, sl=sl, t5=t5, bank=bank, sem_n=sem_n, sem_v=sem_v,
                              x1_done=x1_done, nk=nk, phase=phase):
                            t.wait_ge(B.sems["dS"], max(slab_ld2[sl], x1_done))
                            t.wait_ge(B.sems[sem_n], sem_v)
                            if phase == 2:
                                B.wait(t, "dv")
                            for kc in range(nk):
                                if phase < 2:
                                    rhs = x1T[:, kc * TG + t5 * 512:
                                              kc * TG + t5 * 512 + 512]
                                else:
                                    rhs = sa_h[:, kc * TG + t5 * 512:
                                               kc * TG + t5 * 512 + 512]
                                inst = t.matmul(
                                    pY[bank][:, :],
                                    wsl[sl % 2][:, kc * 128:(kc + 1) * 128],
                                    rhs, start=(kc == 0), stop=(kc == nk - 1))
                            B.inc(inst, "pe")
                        pe_c = B.cnt["pe"]
                        if t5 == 1:
                            slab_pe[sl % 2] = pe_c
                            if sl + 2 < nslab:
                                @block.sync
                                def _(sy, sl=sl, scr=scr, slw=slw):
                                    sy.wait_ge(B.sems["pe"], slab_pe[sl % 2])
                                    B.inc(sy.dma_start(
                                        wsl[sl % 2][:, 0:slw].rearrange(
                                            "p (kc c) -> p kc c", c=128),
                                        scr[sl + 2].rearrange("(kc p) c -> p kc c",
                                                              p=128)), "dS", 16)
                                slab_ld2[sl + 2] = B.cnt["dS"]
                        if phase == 0:
                            @block.scalar
                            def _(s, sl=sl, t5=t5, bank=bank, pe_c=pe_c, g_i=g_i):
                                B.wait(s, "pe", pe_c)
                                if g_i == 0 and sl == 0 and t5 == 0:
                                    B.wait(s, "dv")
                                B.inc(s.activation(
                                    sa_h[:, sl * TG + t5 * 512:sl * TG + t5 * 512 + 512],
                                    pY[bank][:, :], AF.Silu), "ac")
                            bank_free[bank] = ("ac", B.cnt["ac"])
                        elif phase == 1:
                            @block.vector
                            def _(v, sl=sl, t5=t5, bank=bank, pe_c=pe_c):
                                B.wait(v, "pe", pe_c)
                                B.wait(v, "ac")
                                B.inc(v.tensor_tensor(
                                    out=sa_h[:, sl * TG + t5 * 512:
                                             sl * TG + t5 * 512 + 512],
                                    in0=pY[bank][:, :],
                                    in1=sa_h[:, sl * TG + t5 * 512:
                                             sl * TG + t5 * 512 + 512],
                                    op=ALU.mult), "dv")
                            bank_free[bank] = ("dv", B.cnt["dv"])
                        else:
                            @block.vector
                            def _(v, bank=bank, pe_c=pe_c, evf=ev1_free[bank]):
                                B.wait(v, "pe", pe_c)
                                v.wait_ge(B.sems["dG"], evf)
                                B.inc(v.tensor_copy(ev1[bank][:, :],
                                                    pY[bank][:, :]), "dv")
                            ev_dv = B.cnt["dv"]
                            bank_free[bank] = ("dv", ev_dv)
                            @block.gpsimd
                            def _(g, g_i=g_i, sl=sl, t5=t5, bank=bank, ev_dv=ev_dv):
                                B.wait(g, "dv", ev_dv)
                                B.inc(g.dma_start(
                                    out=rs2i3[g_i][t5 * 4:(t5 + 1) * 4,
                                                   sl * 128:(sl + 1) * 128, :]
                                    .rearrange("r p t -> p r t"),
                                    in_=ev1[bank][:, :].rearrange(
                                        "p (r t) -> p r t", r=4)), "dG", 16)
                            ev1_free[bank] = B.cnt["dG"]
                if ffn_phase_taps and phase < 2:
                    if phase == 0:
                        B.wait_marker = B.cnt["ac"]
                        @block.gpsimd
                        def _(g, m=B.cnt["ac"]):
                            B.wait(g, "ac", m)
                            for ti in range(6):
                                w = min(2048, NHT * TG - ti * 2048)
                                B.inc(g.dma_start(
                                    out=ssmtap[ti][:, 0:w],
                                    in_=sa_h[:, ti * 2048:ti * 2048 + w]), "dG", 16)
                    else:
                        @block.gpsimd
                        def _(g, m=B.cnt["dv"]):
                            B.wait(g, "dv", m)
                            for ti in range(6):
                                w = min(2048, NHT * TG - ti * 2048)
                                B.inc(g.dma_start(
                                    out=ssmtap[6 + ti][:, 0:w],
                                    in_=sa_h[:, ti * 2048:ti * 2048 + w]), "dG", 16)
                    @block.vector
                    def _(v, m=B.cnt["dG"]):
                        v.wait_ge(B.sems["dG"], m)
            x1T_pe = B.cnt["pe"]
            ffn_stores = B.cnt["dG"]
            @block.gpsimd
            def _(g, g_i=g_i, ffn_stores=ffn_stores):
                g.wait_ge(B.sems["dG"], ffn_stores)
                B.inc(g.collective_compute(
                    "ReduceScatter", ALU.add,
                    replica_groups=[list(range(NC))],
                    ins=[rs2i[g_i].opt()], outs=[rs2o[g_i].opt()]), "cc3")

        ffn_pe = B.cnt["pe"]

        for g_i in range(ngroups):
            @block.sync
            def _(sy, g_i=g_i, ffn_pe=ffn_pe):
                sy.wait_ge(B.sems["cc3"], ngroups)
                if g_i == 0:
                    sy.wait_ge(B.sems["pe"], ffn_pe)
                B.wait(sy, "dv")
                B.wait(sy, "dG")
                B.inc(sy.dma_start(ln_attn[:, :], rs2o[g_i][0:HID, 0:128],
                                   transpose=True), "dS", 16)
            f_ld = B.cnt["dS"]
            @block.gpsimd
            def _(g, g_i=g_i):
                B.wait(g, "dv")
                B.inc(g.dma_start(out=xc32[:, :], in_=x1tmd[g_i]), "dG", 16)
            x1_ld = B.cnt["dG"]
            @block.vector
            def _(v, f_ld=f_ld, x1_ld=x1_ld, g_i=g_i, ffn_pe=ffn_pe):
                v.wait_ge(B.sems["dS"], f_ld)
                v.wait_ge(B.sems["dG"], x1_ld)
                if g_i == 0:
                    v.wait_ge(B.sems["pe"], ffn_pe)
                v.scalar_tensor_tensor(out=ln_t[:, :], in0=xc32[:, :], scalar=0.0,
                                       in1=ln_attn[:, :], op0=ALU.add, op1=ALU.add)
                v.drain()
                v.reduce_sum(st_s[:, 0:1], ln_t[:, :], axis=mybir.AxisListType.X)
                v.scalar_tensor_tensor(out=ln_sq[:, :], in0=ln_t[:, :], scalar=1.0,
                                       in1=ln_t[:, :], op0=ALU.mult, op1=ALU.mult,
                                       accum_out=st_s[:, 1:2])
                v.drain()
                v.tensor_scalar(out=st_s[:, 2:3], in0=st_s[:, 0:1],
                                scalar1=1.0 / HID, scalar2=None,
                                op0=ALU.mult)
                v.drain()
                v.tensor_tensor(out=st_s[:, 3:4], in0=st_s[:, 2:3],
                                in1=st_s[:, 2:3], op=ALU.mult)
                v.drain()
                B.inc(v.scalar_tensor_tensor(
                    out=st_s[:, 4:5], in0=st_s[:, 1:2],
                    scalar=1.0 / HID, in1=st_s[:, 3:4],
                    op0=ALU.mult, op1=ALU.subtract), "dv")
            var_dv = B.cnt["dv"]
            @block.scalar
            def _(s, var_dv=var_dv):
                B.wait(s, "dv", var_dv)
                B.inc(s.activation(st_s[:, 5:6], st_s[:, 4:5], AF.Ln, bias=EPS), "ac")
            ln_ac = B.cnt["ac"]
            @block.vector
            def _(v, ln_ac=ln_ac):
                B.wait(v, "ac", ln_ac)
                B.inc(v.tensor_scalar(out=st_s[:, 6:7], in0=st_s[:, 5:6],
                                      scalar1=-0.5, scalar2=None, op0=ALU.mult), "dv")
            ln_dv2 = B.cnt["dv"]
            @block.scalar
            def _(s, ln_dv2=ln_dv2):
                B.wait(s, "dv", ln_dv2)
                B.inc(s.activation(st_s[:, 7:8], st_s[:, 6:7], AF.Exp), "ac")
            ln_ac2 = B.cnt["ac"]
            @block.vector
            def _(v, ln_ac2=ln_ac2):
                B.wait(v, "ac", ln_ac2)
                B.inc(v.tensor_scalar(out=ln_o[:, :], in0=ln_t[:, :],
                                      scalar1=st_s[:, 2:3], scalar2=st_s[:, 7:8],
                                      op0=ALU.subtract, op1=ALU.mult), "dv")
            o_dv = B.cnt["dv"]
            @block.gpsimd
            def _(g, g_i=g_i, o_dv=o_dv):
                B.wait(g, "dv", o_dv)
                B.inc(g.dma_start(out=out_x[g_i * 128:(g_i + 1) * 128, :],
                                  in_=ln_o[:, :]), "dG", 16)

        @block.gpsimd
        def _(g):
            B.wait(g, "dG")
            for name, (dst, src) in dbg_out.items():
                B.inc(g.dma_start(out=dst, in_=src), "dG", 16)
            B.wait(g, "dG")

    return B.nc



def _prep_inputs(inputs):
    gi = lambda k: np.ascontiguousarray(np.asarray(inputs[k]))
    x = gi("L0_attn_hidden_states").reshape(T_ALL, HID).astype(np.float32)
    W = gi("L0_attn_in_proj_w").astype(np.float32)
    cw = gi("L0_attn_conv_w")[:, 0, :].astype(np.float32)
    cb = gi("L0_attn_conv_b").astype(np.float32)
    dtb = gi("L0_attn_dt_bias").astype(np.float32)
    alog = gi("L0_attn_A_log").astype(np.float32)
    Dv = gi("L0_attn_D").astype(np.float32)
    nw = gi("L0_attn_norm_w").astype(np.float32)
    Wo = gi("L0_attn_out_proj_w").astype(np.float32)
    w1 = gi("L0_ffn_w1").astype(np.int32)
    s1 = gi("L0_ffn_s1").astype(np.float32)
    w2 = gi("L0_ffn_w2").astype(np.int32)
    s2 = gi("L0_ffn_s2").astype(np.float32)
    w3 = gi("L0_ffn_w3").astype(np.int32)
    s3 = gi("L0_ffn_s3").astype(np.float32)

    s1e = np.repeat(s1, 64).reshape(HIDDEN, HID)
    s3e = np.repeat(s3, 64).reshape(HIDDEN, HID)
    s2e = np.repeat(s2, 64).reshape(HID, HIDDEN)
    w2r = w2.reshape(HID, 86, 64)

    in_maps = []
    for c in range(NC):
        x_rows = W[DIN + 1024 * c:DIN + 1024 * c + 1024]
        B_rows = W[2 * DIN + 128 * c:2 * DIN + 128 * c + 128]
        C_rows = W[2 * DIN + 1024 + 128 * c:2 * DIN + 1024 + 128 * c + 128]
        g_rows = W[1024 * c:1024 * c + 1024]
        dt_rows = np.roll(W[2 * DIN + 2048:2 * DIN + 2048 + 128], -16 * c, axis=0)
        wip = np.ascontiguousarray(
            np.concatenate([x_rows, B_rows, C_rows, g_rows, dt_rows], axis=0))

        cvw_s = np.ascontiguousarray(np.concatenate(
            [cw[1024 * c:1024 * c + 1024], cw[DIN + 128 * c:DIN + 128 * c + 128],
             cw[DIN + 1024 + 128 * c:DIN + 1024 + 128 * c + 128]], axis=0))
        cvb_s = np.ascontiguousarray(np.concatenate(
            [cb[1024 * c:1024 * c + 1024], cb[DIN + 128 * c:DIN + 128 * c + 128],
             cb[DIN + 1024 + 128 * c:DIN + 1024 + 128 * c + 128]],
            axis=0)).reshape(-1, 1)

        dtb_t = np.ascontiguousarray(
            np.tile(np.roll(dtb, -16 * c)[:16], 8)).reshape(1, 128)
        alog_t = np.ascontiguousarray(
            np.tile(np.roll(alog, -16 * c)[:16], 8)).reshape(1, 128)
        drep = np.ascontiguousarray(
            np.repeat(Dv[16 * c:16 * c + 16], 64)).reshape(1, 1024)
        nw_s = np.ascontiguousarray(nw[1024 * c:1024 * c + 1024]).reshape(1, 1024)
        wo_s = np.ascontiguousarray(Wo[:, 1024 * c:1024 * c + 1024])

        h0 = c * HSZ
        h1 = min(h0 + HSZ, HIDDEN)
        nreal = h1 - h0
        w1s = np.zeros((32 * HSZ, 64), np.int32)
        w1s[:32 * nreal] = w1[32 * h0:32 * h1]
        w3s = np.zeros((32 * HSZ, 64), np.int32)
        w3s[:32 * nreal] = w3[32 * h0:32 * h1]
        s1s = np.zeros((HSZ, HID), np.float32)
        s1s[:nreal] = s1e[h0:h1]
        s3s = np.zeros((HSZ, HID), np.float32)
        s3s[:nreal] = s3e[h0:h1]
        q0, q1 = h0 // 128, h1 // 128
        w2s = np.zeros((HID, HSZ // 128, 64), np.int32)
        w2s[:, :q1 - q0] = w2r[:, q0:q1]
        w2s = np.ascontiguousarray(w2s.reshape(HID * HSZ // 128, 64))
        s2s = np.zeros((HID, HSZ), np.float32)
        s2s[:, :nreal] = s2e[:, h0:h1]

        xrow = np.ascontiguousarray(
            x.reshape(NGRP, NC, 128, HID)[:, c].reshape(NGRP * 128, HID))

        in_maps.append(dict(
            x_in=x, xrow_in=xrow, wip_in=wip, wo_in=wo_s, cvw_in=cvw_s,
            cvb_in=cvb_s, dtb_in=dtb_t, alog_in=alog_t, drep_in=drep, nw_in=nw_s,
            w1p_in=w1s, w3p_in=w3s, w2p_in=w2s, s1x_in=s1s, s3x_in=s3s,
            s2x_in=s2s))
    return in_maps


def kernel(**inputs):
    in_maps = _prep_inputs(inputs)
    nc = build_program()
    res = run_bass_kernel_spmd(nc, in_maps, core_ids=list(range(NC)))
    out = np.zeros((NGRP, NC, 128, HID), np.float32)
    for c in range(NC):
        out[:, c] = np.asarray(res.results[c]["out_x"]).reshape(NGRP, 128, HID)
    return out.reshape(2, 4096, HID)
